# version 07 (final)
# baseline (speedup 1.0000x reference)
import math
from contextlib import ExitStack

import numpy as np

import concourse.bass as bass
import concourse.mybir as mybir
from concourse.bass_utils import run_bass_kernel_spmd

F32 = mybir.dt.float32
BF16 = mybir.dt.bfloat16
I32 = mybir.dt.int32
AF = mybir.ActivationFunctionType
ALU = mybir.AluOpType
AX = mybir.AxisListType

NCORES = 8
B, S, E, NH, D = 2, 4096, 4096, 32, 128
I_FF = 11008
TOK = 1024
T = 512
NE = E // 128
NI = I_FF // 128
NS_G = 22
INV_SQRT_D = 1.0 / math.sqrt(D)
LN_EPS = 1e-5
GROUPS = [[0, 1, 2, 3], [4, 5, 6, 7]]

SBUF_BASE = 16512
SBUF_LIMIT = 229344


def ceil32(x):
    return (x + 31) // 32 * 32


def super_shape(ns_total):
    out = []
    j = 0
    while j < ns_total:
        nj = min(4, ns_total - j)
        out.append((j, nj))
        j += nj
    return out


SUP_G = super_shape(NI)
SUP_D = super_shape(NE)


class Ctx:
    def __init__(self, nc, me, eng, sems, cnt):
        self.nc, self.me, self.eng, self.S, self.C = nc, me, eng, sems, cnt

    def dma(self, owner, sem, out, in_, **kw):
        self.C[sem] += 16
        if self.me == owner:
            self.eng.dma_start(out=out, in_=in_, **kw).then_inc(self.S[sem], 16)

    def op(self, owner, fn, sem=None):
        if sem is not None:
            self.C[sem] += 1
        if self.me == owner:
            inst = fn(self.eng)
            if sem is not None:
                inst.then_inc(self.S[sem], 1)

    def collective(self, sem, kind, op, ins, outs):
        self.C[sem] += 1
        if self.me == "gpsimd":
            self.eng.collective_compute(kind, op, replica_groups=GROUPS,
                                        ins=ins, outs=outs).then_inc(self.S[sem], 1)

    def wait(self, waiter, sem, val):
        if val <= 0:
            return
        if self.me == waiter:
            self.eng.wait_ge(self.S[sem], val)


class Layout:
    def __init__(self, nc):
        self.nc = nc
        self.pers_top = SBUF_BASE
        self.arena_base = None
        self.cursor = None
        self.tensors = {}
        self.first_pass = True

    def _alloc(self, name, shape, dtype, off):
        if name in self.tensors:
            return self.tensors[name]
        t = self.nc.alloc_sbuf_tensor_at(name, list(shape), dtype, offset=off)
        self.tensors[name] = t
        return t

    def persistent(self, name, shape, dtype):
        nbytes = ceil32(int(np.prod(shape[1:])) * np.dtype(mybir.dt.np(dtype)).itemsize)
        t = self._alloc(name, shape, dtype, self.pers_top)
        self.pers_top += nbytes
        assert self.pers_top <= SBUF_LIMIT, f"persistent overflow at {name}"
        return t

    def freeze(self):
        self.arena_base = ceil32(self.pers_top)

    def phase_reset(self):
        self.cursor = self.arena_base

    def scratch(self, name, shape, dtype):
        nbytes = ceil32(int(np.prod(shape[1:])) * np.dtype(mybir.dt.np(dtype)).itemsize)
        t = self._alloc(name, shape, dtype, self.cursor)
        self.cursor += nbytes
        assert self.cursor <= SBUF_LIMIT, f"arena overflow {name}: {self.cursor}"
        return t


def build_nc():
    nc = bass.Bass("TRN2", target_bir_lowering=False)
    dd = {}

    def din(name, shape, dtype):
        dd[name] = nc.dram_tensor(name, shape, dtype, kind="ExternalInput")

    din("x_own", [TOK, E], F32)
    din("qw_g", [I_FF, E], I32); din("sc_g", [I_FF, 32], F32); din("zr_g", [I_FF, 32], F32)
    din("qw_u", [I_FF, E], I32); din("sc_u", [I_FF, 32], F32); din("zr_u", [I_FF, 32], F32)
    din("qw_d", [E, I_FF], I32); din("sc_d", [E, 86], F32); din("zr_d", [E, 86], F32)
    din("qkvw", [E, 8 * 384], F32)
    din("qkvb", [8, 384], F32)
    din("outw", [8 * D, E], F32)
    din("outb", [8, 512], F32)
    dd["y_out"] = nc.dram_tensor("y_out", [TOK, E], F32, kind="ExternalOutput")

    dd["wg_t"] = nc.dram_tensor("wg_t", [NE, 128, I_FF], BF16)
    dd["wu_t"] = nc.dram_tensor("wu_t", [NE, 128, I_FF], BF16)
    dd["wd_t"] = nc.dram_tensor("wd_t", [NI, 128, E], BF16)
    dd["y1h"] = nc.dram_tensor("y1h", [TOK, E], F32)
    dd["y2h"] = nc.dram_tensor("y2h", [TOK, E], F32)
    dd["xn3h"] = nc.dram_tensor("xn3h", [TOK, E], BF16)
    dd["ag1_in"] = nc.dram_tensor("ag1_in", [E, TOK], BF16)
    dd["ag1_out"] = nc.dram_tensor("ag1_out", [4 * E, TOK], BF16)
    dd["ctxh"] = nc.dram_tensor("ctxh", [8 * D, S], BF16)
    dd["rs_in"] = nc.dram_tensor("rs_in", [S, E], F32)
    dd["rs_out"] = nc.dram_tensor("rs_out", [TOK, E], F32)

    L = Layout(nc)
    P = {}
    for name, shape, dt in [
        ("ident", [128, 128], BF16), ("spp", [128, 1], F32),
        ("colidx", [128, 128], F32), ("ones_row", [1, 512], F32),
        ("cosb", [128, S], BF16), ("sinb", [128, S], BF16),
        ("masks", [128, 4 * 512], F32), ("invf", [128, 1], F32),
        ("ln_mu", [128, 1], F32), ("ln_nmr", [128, 1], F32),
        ("ln_r", [128, 1], F32), ("ln_t0", [128, 1], F32),
        ("bnst", [128, 48], F32), ("bnag", [128, 2], F32),
        ("psums", [128, 8], F32), ("rcpc", [128, 1], F32),
        ("rswap", [128, 128], BF16),
    ]:
        P[name] = L.persistent(name, shape, dt)
    L.freeze()

    PS = [nc.alloc_psum_tensor(f"ps{i}", [128, 512], F32) for i in range(6)]
    PT = [nc.alloc_psum_tensor(f"pt{i}", [128, 512], BF16) for i in range(2)]

    sem_names = [
        "qa0", "qa1", "scz0", "scz1", "scz2", "scz3", "wwr0", "wwr1",
        "wc0", "wc1", "wc2", "wc3", "xl0", "xl1",
        "yw0", "yw1", "yw2", "yw3", "agw", "ctw",
        "l20", "l21", "qkw", "cf", "ow", "ob", "cc",
        "tp", "mm", "evv", "evs", "evg",
    ]
    sems = {}
    stack = ExitStack()
    for n in sem_names:
        sems[n] = stack.enter_context(nc.semaphore("sm_" + n))

    with nc.Block() as block:
        def body(me):
            def f(eng):
                cnt = {n: 0 for n in sem_names}
                ctx = Ctx(nc, me, eng, sems, cnt)
                emit_all(ctx, dd, P, L, PS, PT)
            return f
        block.sync(body("sync"))
        block.gpsimd(body("gpsimd"))
        block.vector(body("vector"))
        block.scalar(body("scalar"))
        block.tensor(body("tensor"))
    stack.close()
    return nc



def new_state():
    return dict(
        q_i=0, q_cons={0: {}, 1: {}},
        scz_cons={i: {} for i in range(4)},
        dqs_tp=0,
        wtmp_i=0, wtmp_prev={0: 0, 1: 0},
        wst_i=0, wst_cons={i: 0 for i in range(4)},
        ps_free={i: None for i in range(8)},
        pt_free={0: 0, 1: 0},
        xl_i=0, xl_cons={0: {}, 1: {}},
        yw_i=0,
        l2_i=0, l2_cons={0: 0, 1: 0},
    )


def gcopy(e, dst, srcap):
    if hasattr(e, "tensor_copy"):
        return e.tensor_copy(dst, srcap)
    return e.copy(dst, srcap)


def psum_wait(c, st, owner, idx):
    fr = st["ps_free"][idx]
    if fr is not None:
        c.wait(owner, fr[0], fr[1])


def psum_set_free(c, st, idx, sem):
    st["ps_free"][idx] = (sem, c.C[sem])


def wait_marks(c, owner, marks):
    for semn, val in marks.items():
        c.wait(owner, semn, val)



def emit_prologue(ctx, dd, P):
    c = ctx
    c.op("gpsimd", lambda e: e.iota(P["colidx"][:, :], [[1, 128]], channel_multiplier=0,
                                    allow_small_or_imprecise_dtypes=True), "evg")
    c.op("gpsimd", lambda e: e.iota(P["spp"][:, :], [[1, 1]], channel_multiplier=1,
                                    allow_small_or_imprecise_dtypes=True), "evg")
    c.op("gpsimd", lambda e: e.memset(P["ones_row"][:, :], 1.0), "evg")
    c.op("gpsimd", lambda e: e.iota(P["masks"][:, 0:512], [[1, 512]], base=0,
                                    channel_multiplier=-1,
                                    allow_small_or_imprecise_dtypes=True), "evg")
    c.op("gpsimd", lambda e: e.iota(P["invf"][:, :], [[1, 1]], channel_multiplier=1,
                                    allow_small_or_imprecise_dtypes=True), "evg")
    c.op("gpsimd", lambda e: e.iota(P["rope32"][:, :], [[1, S]], channel_multiplier=0,
                                    allow_small_or_imprecise_dtypes=True), "evg")
    gA = c.C["evg"]
    c.wait("vector", "evg", gA)
    c.op("vector", lambda e: e.tensor_scalar(P["ident"][:, :], P["colidx"][:, :],
                                             P["spp"][:, :1], None, ALU.is_equal), "evv")
    for phi in range(3, -1, -1):
        c.op("vector", lambda e, phi=phi: e.tensor_scalar(
            P["masks"][:, phi * 512:(phi + 1) * 512], P["masks"][:, 0:512],
            float(0.5 + 128 * phi), -66504.0, ALU.is_ge, ALU.mult), "evv")
    c.op("vector", lambda e: e.tensor_scalar(
        P["ln_t0"][:, :], P["invf"][:, :], 63.5, 64.0, ALU.is_gt, ALU.mult), "evv")
    c.wait("vector", "evv", c.C["evv"])
    c.op("vector", lambda e: e.tensor_sub(P["invf"][:, :], P["invf"][:, :],
                                          P["ln_t0"][:, :]), "evv")
    vJ = c.C["evv"]
    c.wait("scalar", "evv", vJ)
    c.op("scalar", lambda e: e.activation(P["invf"][:, :], P["invf"][:, :], AF.Exp,
                                          scale=-math.log(10000.0) / 64.0), "evs")
    sI = c.C["evs"]
    c.wait("vector", "evs", sI)
    c.op("vector", lambda e: e.tensor_scalar(P["rope32"][:, :], P["rope32"][:, :],
                                             P["invf"][:, :1], None, ALU.mult), "evv")
    c.wait("vector", "evv", c.C["evv"])
    c.op("vector", lambda e: e.tensor_scalar(P["rope32b"][:, :], P["rope32"][:, :],
                                             1.0 / (2 * math.pi), 0.5,
                                             ALU.mult, ALU.add), "evv")
    c.wait("vector", "evv", c.C["evv"])
    i32v = P["rope32b"].ap().bitcast(I32)
    c.op("vector", lambda e: e.tensor_copy(i32v[:, :], P["rope32b"][:, :]), "evv")
    c.wait("vector", "evv", c.C["evv"])
    c.op("vector", lambda e: e.tensor_copy(P["rope32b"][:, :], i32v[:, :]), "evv")
    c.wait("vector", "evv", c.C["evv"])
    c.op("vector", lambda e: e.scalar_tensor_tensor(
        P["rope32"][:, :], P["rope32b"][:, :], -2 * math.pi, P["rope32"][:, :],
        ALU.mult, ALU.add), "evv")
    vR2 = c.C["evv"]
    c.wait("scalar", "evv", vR2)
    c.op("scalar", lambda e: e.activation(P["rope32b"][:, :], P["rope32"][:, :],
                                          AF.Sin), "evs")
    c.wait("scalar", "evs", c.C["evs"])
    c.op("scalar", lambda e: e.activation(P["rope32"][:, :], P["rope32"][:, :],
                                          AF.Abs), "evs")
    sS = c.C["evs"]
    c.wait("vector", "evs", sS)
    c.op("vector", lambda e: e.tensor_scalar(
        P["rope32"][:, :], P["rope32"][:, :],
        math.pi / 2, -1.0, ALU.subtract, ALU.mult), "evv")
    vC = c.C["evv"]
    c.wait("scalar", "evv", vC)
    c.op("scalar", lambda e: e.activation(P["rope32"][:, :], P["rope32"][:, :],
                                          AF.Sin), "evs")
    s2 = c.C["evs"]
    c.wait("vector", "evs", s2)
    c.op("vector", lambda e: e.tensor_copy(P["cosb"][:, :], P["rope32"][:, :]), "evv")
    c.op("vector", lambda e: e.tensor_scalar(P["sinb"][0:64, :], P["rope32b"][0:64, :],
                                             -1.0, None, ALU.mult), "evv")
    c.op("vector", lambda e: e.tensor_copy(P["sinb"][64:128, :],
                                           P["rope32b"][64:128, :]), "evv")
    c.wait("vector", "evv", c.C["evv"])
    c.op("vector", lambda e: e.tensor_scalar(P["rope32"][:, 0:128], P["colidx"][:, :],
                                             P["spp"][:, :1], None, ALU.subtract), "evv")
    c.wait("scalar", "evv", c.C["evv"])
    c.op("scalar", lambda e: e.activation(P["rope32"][:, 0:128], P["rope32"][:, 0:128],
                                          AF.Abs), "evs")
    c.wait("vector", "evs", c.C["evs"])
    c.op("vector", lambda e: e.tensor_scalar(P["rswap"][:, :], P["rope32"][:, 0:128],
                                             64.0, None, ALU.is_equal), "evv")



def emit_deq_quad(c, dd, P, f, ps, pt, mat, jq, nj, st):
    if mat == "d":
        qw, sc, zr, wt = dd["qw_d"], dd["sc_d"], dd["zr_d"], dd["wd_t"]
        nin, ngr = I_FF, 86
    elif mat == "g":
        qw, sc, zr, wt = dd["qw_g"], dd["sc_g"], dd["zr_g"], dd["wg_t"]
        nin, ngr = E, 32
    else:
        qw, sc, zr, wt = dd["qw_u"], dd["sc_u"], dd["zr_u"], dd["wu_t"]
        nin, ngr = E, 32
    KG = 8
    nkg = (ngr + KG - 1) // KG
    lanes = ("vector", "scalar", "gpsimd")
    lane_sem = {"vector": "evv", "scalar": "evs", "gpsimd": "evg"}

    scz_vals = []
    zs_vals = []
    for jo in range(nj):
        j = jq + jo
        sl = jo
        wait_marks(c, "sync", st["scz_cons"][sl])
        base = sl * 288
        c.dma("sync", f"scz{sl}", f["sczs"][:, base:base + ngr],
              sc.ap()[j * 128:(j + 1) * 128, :])
        c.dma("sync", f"scz{sl}", f["sczs"][:, base + 96:base + 96 + ngr],
              zr.ap()[j * 128:(j + 1) * 128, :])
        sv = c.C[f"scz{sl}"]
        scz_vals.append(sv)
        c.wait("vector", f"scz{sl}", sv)
        c.op("vector", lambda e, base=base, ngr=ngr: e.scalar_tensor_tensor(
            f["sczs"][:, base + 192:base + 192 + ngr],
            f["sczs"][:, base + 96:base + 96 + ngr], -1.0,
            f["sczs"][:, base:base + ngr], ALU.mult, ALU.mult), "evv")
        zs_vals.append(c.C["evv"])

    for kg in range(nkg):
        g0 = kg * KG
        ng_here = min(KG, ngr - g0)
        for jo in range(nj):
            j = jq + jo
            sl = jo
            qs = st["q_i"] % 2
            wait_marks(c, "gpsimd", st["q_cons"][qs])
            c.dma("gpsimd", f"qa{qs}", f["qbuf"][:, qs * 1024:qs * 1024 + ng_here * 128],
                  qw.ap()[j * 128:(j + 1) * 128, g0 * 128:(g0 + ng_here) * 128])
            qa = c.C[f"qa{qs}"]
            newcons = {}
            for gg in range(ng_here):
                g = g0 + gg
                own = lanes[(g + j) % 3]
                sem = lane_sem[own]
                c.wait(own, f"qa{qs}", qa)
                c.wait(own, "tp", st["dqs_tp"])
                base = sl * 288
                if own == "scalar":
                    c.wait(own, "evv", zs_vals[jo])
                    c.op(own, lambda e, jo=jo, gg=gg, qs=qs, base=base, g=g: e.activation(
                        f["dqs"][:, jo * 1024 + gg * 128:jo * 1024 + (gg + 1) * 128],
                        f["qbuf"][:, qs * 1024 + gg * 128:qs * 1024 + (gg + 1) * 128],
                        AF.Identity,
                        bias=f["sczs"][:, base + 192 + g:base + 192 + g + 1],
                        scale=f["sczs"][:, base + g:base + g + 1]), sem)
                else:
                    c.wait(own, f"scz{sl}", scz_vals[jo])
                    c.op(own, lambda e, jo=jo, gg=gg, qs=qs, base=base, g=g: e.tensor_scalar(
                        f["dqs"][:, jo * 1024 + gg * 128:jo * 1024 + (gg + 1) * 128],
                        f["qbuf"][:, qs * 1024 + gg * 128:qs * 1024 + (gg + 1) * 128],
                        f["sczs"][:, base + 96 + g:base + 96 + g + 1],
                        f["sczs"][:, base + g:base + g + 1],
                        ALU.subtract, ALU.mult), sem)
                newcons[sem] = c.C[sem]
                st["scz_cons"][sl][sem] = c.C[sem]
            st["q_cons"][qs] = dict(newcons)
            st["q_i"] += 1
        cells_done = {sem: c.C[sem] for sem in ("evv", "evs", "evg")}
        for gg in range(ng_here):
            k = g0 + gg
            ptb = st["wtmp_i"] % 2
            tslot = ptb
            eng = "vector" if ptb == 0 else "scalar"
            esem = lane_sem[eng]
            wsem = "wwr0" if tslot == 0 else "wwr1"
            if gg == 0:
                wait_marks(c, "tensor", cells_done)
            c.wait("tensor", "evv" if ptb == 0 else "evs", st["pt_free"][ptb])
            for jo in range(nj):
                c.op("tensor", lambda e, ptb=ptb, jo=jo, gg=gg: e.transpose(
                    pt[ptb][:, jo * 128:(jo + 1) * 128],
                    f["dqs"][:, jo * 1024 + gg * 128:jo * 1024 + (gg + 1) * 128],
                    P["ident"][:, :]), "tp" if jo == nj - 1 else None)
            tpv = c.C["tp"]
            c.wait(eng, "tp", tpv)
            c.wait(eng, wsem, st["wtmp_prev"][tslot])
            c.op(eng, lambda e, tslot=tslot, ptb=ptb, nj=nj: gcopy(
                e, f["wtmp"][:, tslot * 512:tslot * 512 + nj * 128],
                pt[ptb][:, 0:nj * 128]), esem)
            st["pt_free"][ptb] = c.C[esem]
            ev = c.C[esem]
            c.wait("gpsimd", esem, ev)
            c.dma("gpsimd", wsem, wt.ap()[k, :, jq * 128:jq * 128 + nj * 128],
                  f["wtmp"][:, tslot * 512:tslot * 512 + nj * 128])
            st["wtmp_prev"][tslot] = c.C[wsem]
            st["wtmp_i"] += 1
        st["dqs_tp"] = c.C["tp"]



def emit_ln_core(c, P, f_slab0, f_slab1, xv0, xv1, xntm, st):
    c.wait("vector", xv0[0], xv0[1])
    c.wait("vector", xv1[0], xv1[1])
    for h8 in range(8):
        slab = f_slab0 if h8 < 4 else f_slab1
        off = (h8 % 4) * 512
        c.op("vector", lambda e, slab=slab, off=off, h8=h8: e.bn_stats(
            P["bnst"][:, h8 * 6:(h8 + 1) * 6], slab[:, off:off + 512]), "evv")
    c.wait("vector", "evv", c.C["evv"])
    c.op("vector", lambda e: e.bn_aggr(P["bnag"][:, :], P["bnst"][:, :]), "evv")
    c.wait("vector", "evv", c.C["evv"])
    c.op("vector", lambda e: e.tensor_scalar(
        P["ln_t0"][:, :], P["bnag"][:, 1:2], LN_EPS, None, ALU.add), "evv")
    c.wait("scalar", "evv", c.C["evv"])
    c.op("scalar", lambda e: e.activation(P["ln_t0"][:, :], P["ln_t0"][:, :],
                                          AF.Sqrt), "evs")
    c.wait("vector", "evs", c.C["evs"])
    c.op("vector", lambda e: e.reciprocal(P["ln_r"][:, :], P["ln_t0"][:, :]), "evv")
    c.wait("vector", "evv", c.C["evv"])
    c.op("vector", lambda e: e.scalar_tensor_tensor(
        P["ln_nmr"][:, :], P["bnag"][:, 0:1], -1.0, P["ln_r"][:, :],
        ALU.mult, ALU.mult), "evv")
    c.wait("scalar", "evv", c.C["evv"])
    c.wait("scalar", "tp", st.get("xntm_tp", 0))
    c.wait("scalar", xv0[0], xv0[1])
    c.wait("scalar", xv1[0], xv1[1])
    for hh, slab in ((0, f_slab0), (1, f_slab1)):
        c.op("scalar", lambda e, hh=hh, slab=slab: e.activation(
            xntm[:, hh * 2048:(hh + 1) * 2048], slab[:, :], AF.Identity,
            bias=P["ln_nmr"][:, :1], scale=P["ln_r"][:, :1]), "evs")
    return c.C["evs"]


def emit_tp_packs(c, P, f, ps, pt, st, n_packs, src_fn, dst_fn, extra_evac_waits=()):
    for n in range(n_packs):
        ptb = st["wtmp_i"] % 2
        eng = "vector" if ptb == 0 else "scalar"
        esem = {"vector": "evv", "scalar": "evs"}[eng]
        c.wait("tensor", "evv" if ptb == 0 else "evs", st["pt_free"][ptb])
        for i4 in range(4):
            c.op("tensor", lambda e, ptb=ptb, n=n, i4=i4: e.transpose(
                pt[ptb][:, i4 * 128:(i4 + 1) * 128], src_fn(n, i4),
                P["ident"][:, :]), "tp" if i4 == 3 else None)
        tpv = c.C["tp"]
        c.wait(eng, "tp", tpv)
        for semn, val in extra_evac_waits:
            c.wait(eng, semn, val)
        dst = dst_fn(n)
        c.op(eng, lambda e, ptb=ptb, dst=dst: gcopy(
            e, dst, pt[ptb][:, 0:512] if dst.ndim == 2 else
            pt[ptb][:, 0:512].rearrange("p (k t) -> p k t", t=128)), esem)
        st["pt_free"][ptb] = c.C[esem]
        st["wtmp_i"] += 1
    return c.C["tp"]


def emit_xnt_half(c, dd, P, f, ps, pt, st, src, half, do_ln):
    for tc in range(4):
        row0 = half * T + tc * 128
        slab0 = f["xys"][:, 0:2048]
        slab1 = f["xys"][:, 2048:4096]
        if do_ln:
            wait_marks(c, "sync", st["xl_cons"][0])
            c.dma("sync", "xl0", slab0, src.ap()[row0:row0 + 128, 0:2048])
            wait_marks(c, "sync", st["xl_cons"][1])
            c.dma("sync", "xl1", slab1, src.ap()[row0:row0 + 128, 2048:4096])
            apv = emit_ln_core(c, P, slab0, slab1, ("xl0", c.C["xl0"]),
                               ("xl1", c.C["xl1"]), f["xntm"], st)
            st["xl_cons"][0] = {"evs": apv, "evv": c.C["evv"]}
            st["xl_cons"][1] = {"evs": apv, "evv": c.C["evv"]}
            ap_done = ("evs", apv)
        else:
            wait_marks(c, "sync", st["xl_cons"][0])
            c.wait("sync", "agw", st.get("xntm3_agw", 0))
            c.wait("sync", "tp", st.get("xntm_tp", 0))
            c.dma("sync", "xl0", f["xntm"][:, :], src.ap()[row0:row0 + 128, :])
            ap_done = ("xl0", c.C["xl0"])
            st["xl_cons"][0] = {}

        def src_fn(eq, i4):
            ec = eq * 4 + i4
            return f["xntm"][:, ec * 128:(ec + 1) * 128]

        def dst_fn(eq):
            return f["xnt"][:, :].rearrange("p (k t) -> p k t", t=T)[
                :, 4 * eq:4 * eq + 4, tc * 128:(tc + 1) * 128]

        c.wait("tensor", ap_done[0], ap_done[1])
        st["xntm_tp"] = emit_tp_packs(c, P, f, ps, pt, st, 8, src_fn, dst_fn,
                                      extra_evac_waits=(("mm", st.get("xnt_cons_mm", 0)),))



def emit_gate_up_sweep(c, dd, P, f, ps, pt, st, mat, xnt_ready):
    wt = dd["wg_t"] if mat == "g" else dd["wu_t"]
    for (jq, nj) in SUP_G:
        sidx = jq // 4
        for k in range(NE):
            slot = st["wst_i"] % 4
            c.wait("sync", "mm", st["wst_cons"][slot])
            c.wait("sync", "wwr0", c.C["wwr0"])
            c.wait("sync", "wwr1", c.C["wwr1"])
            c.dma("sync", f"wc{slot}", f["wst"][:, slot * 512:slot * 512 + nj * 128],
                  wt.ap()[k, :, jq * 128:jq * 128 + nj * 128])
            wv = c.C[f"wc{slot}"]
            c.wait("tensor", f"wc{slot}", wv)
            if k == 0:
                for m in range(nj):
                    psum_wait(c, st, "tensor", m)
                for semn, val in xnt_ready.items():
                    c.wait("tensor", semn, val)
            for m in range(nj):
                c.op("tensor", lambda e, m=m, slot=slot, k=k, nj=nj: e.matmul(
                    ps[m][:, :], f["wst"][:, slot * 512 + m * 128:slot * 512 + (m + 1) * 128],
                    f["xnt"][:, k * T:(k + 1) * T],
                    start=(k == 0), stop=(k == NE - 1)),
                    "mm" if m == nj - 1 else None)
            st["wst_cons"][slot] = c.C["mm"]
            st["wst_i"] += 1
        mmv = c.C["mm"]
        for m in range(nj):
            jj = jq + m
            if mat == "g":
                c.wait("scalar", "mm", mmv)
                c.wait("scalar", "mm", st.get("hbuf_cons_mm", 0))
                c.op("scalar", lambda e, m=m, jj=jj: e.activation(
                    f["hbuf"][:, jj * T:(jj + 1) * T], ps[m][:, :], AF.Silu), "evs")
                psum_set_free(c, st, m, "evs")
            else:
                c.wait("vector", "mm", mmv)
                c.wait("vector", "evs", st.get("gate_silu_done", 0))
                c.op("vector", lambda e, m=m, jj=jj: e.scalar_tensor_tensor(
                    f["hbuf"][:, jj * T:(jj + 1) * T], ps[m][:, :], 1.0,
                    f["hbuf"][:, jj * T:(jj + 1) * T], ALU.mult, ALU.mult), "evv")
                psum_set_free(c, st, m, "evv")


def emit_down_sweep(c, dd, P, f, ps, pt, st, half, xsrc, ydst, ysem_ring, h_ready):
    for dc in range(8):
        for k in range(NI):
            slot = st["wst_i"] % 4
            c.wait("sync", "mm", st["wst_cons"][slot])
            c.wait("sync", "wwr0", c.C["wwr0"])
            c.wait("sync", "wwr1", c.C["wwr1"])
            c.dma("sync", f"wc{slot}", f["wst"][:, slot * 512:(slot + 1) * 512],
                  dd["wd_t"].ap()[k, :, dc * 512:(dc + 1) * 512])
            wv = c.C[f"wc{slot}"]
            c.wait("tensor", f"wc{slot}", wv)
            if k == 0:
                for tcb in range(4):
                    psum_wait(c, st, "tensor", tcb)
                for semn, val in h_ready.items():
                    c.wait("tensor", semn, val)
            for tcb in range(4):
                c.op("tensor", lambda e, tcb=tcb, slot=slot, k=k: e.matmul(
                    ps[tcb][:, :],
                    f["hbuf"][:, k * T + tcb * 128:k * T + (tcb + 1) * 128],
                    f["wst"][:, slot * 512:(slot + 1) * 512],
                    start=(k == 0), stop=(k == NI - 1)),
                    "mm" if tcb == 3 else None)
            st["wst_cons"][slot] = c.C["mm"]
            st["wst_i"] += 1
        mmv = c.C["mm"]
        for tcb in range(4):
            row0 = half * T + tcb * 128
            ysl = st["yw_i"] % 4
            xsl = st["xl_i"] % 2
            wait_marks(c, "sync", st["xl_cons"][xsl])
            c.dma("sync", f"xl{xsl}", f["xys"][:, xsl * 2048:xsl * 2048 + 512],
                  xsrc.ap()[row0:row0 + 128, dc * 512:(dc + 1) * 512])
            xv = c.C[f"xl{xsl}"]
            st["xl_i"] += 1
            c.wait("vector", "mm", mmv)
            c.wait("vector", f"xl{xsl}", xv)
            c.wait("vector", f"yw{ysl}", st["yw_prev"][ysl])
            c.op("vector", lambda e, tcb=tcb, ysl=ysl, xsl=xsl: e.scalar_tensor_tensor(
                f["ytile"][:, ysl * 512:(ysl + 1) * 512], ps[tcb][:, :], 0.5,
                f["xys"][:, xsl * 2048:xsl * 2048 + 512], ALU.mult, ALU.add), "evv")
            ev = c.C["evv"]
            st["xl_cons"][xsl] = {"evv": ev}
            psum_set_free(c, st, tcb, "evv")
            c.wait("gpsimd", "evv", ev)
            c.dma("gpsimd", f"yw{ysl}", ydst.ap()[row0:row0 + 128, dc * 512:(dc + 1) * 512],
                  f["ytile"][:, ysl * 512:(ysl + 1) * 512])
            st["yw_prev"][ysl] = c.C[f"yw{ysl}"]
            st["yw_i"] += 1



def emit_ln_pass(c, dd, P, f, ps, pt, st, src, half):
    for i in range(4):
        c.wait("sync", f"yw{i}", c.C[f"yw{i}"])
    for tc in range(4):
        row0 = half * T + tc * 128
        slab0 = f["xys"][:, 0:2048]
        slab1 = f["xys"][:, 2048:4096]
        wait_marks(c, "sync", st["xl_cons"][0])
        c.dma("sync", "xl0", slab0, src.ap()[row0:row0 + 128, 0:2048])
        wait_marks(c, "sync", st["xl_cons"][1])
        c.dma("sync", "xl1", slab1, src.ap()[row0:row0 + 128, 2048:4096])
        apv = emit_ln_core(c, P, slab0, slab1, ("xl0", c.C["xl0"]),
                           ("xl1", c.C["xl1"]), f["xntm"], st)
        st["xl_cons"][0] = {"evs": apv, "evv": c.C["evv"]}
        st["xl_cons"][1] = {"evs": apv, "evv": c.C["evv"]}
        col0 = (half * 4 + tc) * 128
        packno = [0]

        def src_fn(eq, i4):
            ec = eq * 4 + i4
            return f["xntm"][:, ec * 128:(ec + 1) * 128]

        for eq in range(8):
            ptb = st["wtmp_i"] % 2
            tsl = ptb
            wsem = "wwr0" if tsl == 0 else "wwr1"
            eng = "vector" if ptb == 0 else "scalar"
            esem = {"vector": "evv", "scalar": "evs"}[eng]
            c.wait("tensor", "evs", apv)
            c.wait("tensor", "evv" if ptb == 0 else "evs", st["pt_free"][ptb])
            for i4 in range(4):
                c.op("tensor", lambda e, ptb=ptb, eq=eq, i4=i4: e.transpose(
                    pt[ptb][:, i4 * 128:(i4 + 1) * 128], src_fn(eq, i4),
                    P["ident"][:, :]), "tp" if i4 == 3 else None)
            tpv = c.C["tp"]
            c.wait(eng, "tp", tpv)
            c.wait(eng, "agw", c.C["agw"])
            c.wait(eng, wsem, st["wtmp_prev"][tsl])
            c.op(eng, lambda e, ptb=ptb, tsl=tsl: gcopy(
                e, f["wtmp"][:, tsl * 512:(tsl + 1) * 512], pt[ptb][:, :]), esem)
            st["pt_free"][ptb] = c.C[esem]
            ev = c.C[esem]
            c.wait("gpsimd", esem, ev)
            dst = bass.AP(dd["ag1_in"], eq * 4 * 128 * TOK + col0,
                          [[TOK, 128], [128 * TOK, 4], [1, 128]])
            c.dma("gpsimd", "agw", dst, f["wtmp"][:, tsl * 512:(tsl + 1) * 512]
                  .rearrange("p (k t) -> p k t", t=128))
            st["agev_prev"][tsl] = c.C["agw"]
            st["wtmp_i"] += 1
        st["xntm_tp"] = c.C["tp"]



def emit_ffn_block(c, dd, P, f, ps, pt, st, first):
    xsrc = dd["x_own"] if first else dd["y2h"]
    ydst = dd["y1h"] if first else dd["y_out"]
    for half in (0, 1):
        if first:
            emit_xnt_half(c, dd, P, f, ps, pt, st, xsrc, half, do_ln=True)
        else:
            emit_xnt_half(c, dd, P, f, ps, pt, st, dd["xn3h"], half, do_ln=False)
        xnt_ready = {"evv": c.C["evv"], "evs": c.C["evs"]}
        if first and half == 0:
            for (jq, nj) in SUP_G:
                emit_deq_quad(c, dd, P, f, ps, pt, "g", jq, nj, st)
            emit_gate_up_sweep(c, dd, P, f, ps, pt, st, "g", xnt_ready)
            st["gate_silu_done"] = c.C["evs"]
            for (jq, nj) in SUP_G:
                emit_deq_quad(c, dd, P, f, ps, pt, "u", jq, nj, st)
            emit_gate_up_sweep(c, dd, P, f, ps, pt, st, "u", xnt_ready)
            st["xnt_cons_mm"] = c.C["mm"]
            for (jq, nj) in SUP_D:
                emit_deq_quad(c, dd, P, f, ps, pt, "d", jq, nj, st)
            h_ready = {"evv": c.C["evv"], "evs": c.C["evs"]}
            emit_down_sweep(c, dd, P, f, ps, pt, st, half, xsrc, ydst, None, h_ready)
            st["hbuf_cons_mm"] = c.C["mm"]
        else:
            emit_gate_up_sweep(c, dd, P, f, ps, pt, st, "g", xnt_ready)
            st["gate_silu_done"] = c.C["evs"]
            emit_gate_up_sweep(c, dd, P, f, ps, pt, st, "u", xnt_ready)
            st["xnt_cons_mm"] = c.C["mm"]
            h_ready = {"evv": c.C["evv"], "evs": c.C["evs"]}
            emit_down_sweep(c, dd, P, f, ps, pt, st, half, xsrc, ydst, None, h_ready)
            st["hbuf_cons_mm"] = c.C["mm"]
        if first:
            emit_ln_pass(c, dd, P, f, ps, pt, st, dd["y1h"], half)



def emit_attention(c, dd, P, a, ps, pt, st):
    for hl in range(8):
        c.wait("gpsimd", "mm", st["qk_prev_mm"])
        for part in range(4):
            src = bass.AP(dd["qkvw"], (part * 8) * 128 * (8 * 384) + hl * 384,
                          [[8 * 384, 128], [128 * 8 * 384, 8], [1, 384]])
            c.dma("gpsimd", "qkw", a["qkvw_sb"][:, :].rearrange(
                "p (k c) -> p k c", c=384)[:, part * 8:(part + 1) * 8, :], src)
        qkwv = c.C["qkw"]
        c.wait("sync", "mm", st["qk_prev_mm"])
        c.dma("sync", "ob", a["qkvb_sb"][:, :], dd["qkvb"].ap()[hl:hl + 1, :])
        obv = c.C["ob"]
        for tc in range(8):
            c.wait("sync", "mm", st["l2_cons"][0])
            c.wait("sync", "mm", st["l2_cons"][1])
            sl = st["l2_i"] % 2
            c.wait("sync", "cc", st["cc_need"])
            quarter, lo = tc // 2, (tc % 2) * 512
            src = bass.AP(dd["ag1_out"], quarter * E * TOK + lo,
                          [[TOK, 128], [128 * TOK, NE], [1, 512]])
            c.dma("sync", f"l2{sl}", a["l2slab"][:, :].rearrange(
                "p (k t) -> p k t", t=512), src)
            lv = c.C[f"l2{sl}"]
            st["l2_i"] += 1
            for cc in range(3):
                pb = cc
                c.wait("tensor", f"l2{sl}", lv)
                c.wait("tensor", "qkw", qkwv)
                c.wait("tensor", "ob", obv)
                psum_wait(c, st, "tensor", pb)
                for e_ in range(NE):
                    c.op("tensor", lambda e, pb=pb, e_=e_, cc=cc, sl=sl: e.matmul(
                        ps[pb][:, :],
                        a["qkvw_sb"][:, e_ * 384 + cc * 128:e_ * 384 + (cc + 1) * 128],
                        a["l2slab"][:, e_ * 512:(e_ + 1) * 512],
                        start=(e_ == 0), stop=False), None)
                c.op("tensor", lambda e, pb=pb, cc=cc: e.matmul(
                    ps[pb][:, :], a["qkvb_sb"][0:1, cc * 128:(cc + 1) * 128],
                    P["ones_row"][0:1, :], start=False, stop=True), "mm")
                mmv = c.C["mm"]
                c.wait("scalar", "mm", mmv)
                c.op("scalar", lambda e, pb=pb, cc=cc, tc=tc: e.copy(
                    a["qkv_sb"][:, cc * S + tc * 512:cc * S + (tc + 1) * 512],
                    ps[pb][:, :]), "evs")
                psum_set_free(c, st, pb, "evs")
            st["l2_cons"][sl] = c.C["mm"]
        st["qk_prev_mm"] = c.C["mm"]
        qkv_done = c.C["evs"]
        rt = a["probsT"]
        c.wait("vector", "evs", qkv_done)
        c.wait("vector", "mm", st.get("pv_mm", 0))
        for cc in (0, 1):
            X = a["qkv_sb"][:, cc * S:(cc + 1) * S]
            c.wait("vector", "evv", c.C["evv"])
            c.op("vector", lambda e, X=X: e.tensor_mul(rt[:, 0:S], X, P["cosb"][:, :]), "evv")
            for tch in range(8):
                psum_wait(c, st, "tensor", 3)
                c.wait("tensor", "evs", qkv_done)
                c.op("tensor", lambda e, X=X, tch=tch: e.matmul(
                    ps[3][:, :], P["rswap"][:, :], X[:, tch * 512:(tch + 1) * 512],
                    start=True, stop=True), "mm")
                swv = c.C["mm"]
                c.wait("vector", "mm", swv)
                c.op("vector", lambda e, tch=tch: e.scalar_tensor_tensor(
                    rt[:, S + tch * 512:S + (tch + 1) * 512], ps[3][:, :], 1.0,
                    P["sinb"][:, tch * 512:(tch + 1) * 512], ALU.mult, ALU.mult), "evv")
                psum_set_free(c, st, 3, "evv")
            c.wait("vector", "evv", c.C["evv"])
            c.op("vector", lambda e, X=X: e.tensor_add(X, rt[:, 0:S], rt[:, S:2 * S]), "evv")
        rope_done = c.C["evv"]
        for kk in range(8):
            ptb = st["wtmp_i"] % 2
            c.wait("tensor", "evs", qkv_done)
            if st.get("pt_free"):
                c.wait("tensor", "evv" if ptb == 0 else "evs", st["pt_free"][ptb])
            for i4 in range(4):
                tkc = kk * 4 + i4
                c.op("tensor", lambda e, ptb=ptb, i4=i4, tkc=tkc: e.transpose(
                    pt[ptb][:, i4 * 128:(i4 + 1) * 128],
                    a["qkv_sb"][:, 2 * S + tkc * 128:2 * S + (tkc + 1) * 128],
                    P["ident"][:, :]), "tp" if i4 == 3 else None)
            tpv = c.C["tp"]
            eng = "vector" if ptb == 0 else "scalar"
            esem = {"vector": "evv", "scalar": "evs"}[eng]
            c.wait(eng, "tp", tpv)
            c.wait(eng, "mm", st.get("pv_mm", 0))
            c.op(eng, lambda e, ptb=ptb, kk=kk: gcopy(
                e, a["vT"][:, kk * 512:(kk + 1) * 512], pt[ptb][:, :]), esem)
            st["pt_free"][ptb] = c.C[esem]
            st["wtmp_i"] += 1
        vt_done = {"evv": c.C["evv"], "evs": c.C["evs"]}
        q = a["qkv_sb"][:, 0:S]
        k_ = a["qkv_sb"][:, S:2 * S]
        for C4 in range(8):
            ntk = C4 + 1
            nkk = 4 * ntk
            for tq4 in range(4):
                tqc = C4 * 4 + tq4
                for tkb in range(ntk):
                    pb = 3 + (tkb % 2)
                    psum_wait(c, st, "tensor", pb)
                    c.wait("tensor", "evv", rope_done)
                    c.op("tensor", lambda e, pb=pb, tqc=tqc, tkb=tkb: e.matmul(
                        ps[pb][:, :], q[:, tqc * 128:(tqc + 1) * 128],
                        k_[:, tkb * 512:(tkb + 1) * 512],
                        start=True, stop=True), "mm")
                    smv = c.C["mm"]
                    if tkb == ntk - 1:
                        c.wait("vector", "mm", smv)
                        c.wait("vector", "evs", st.get("diag_evs", 0))
                        c.op("vector", lambda e, pb=pb, tq4=tq4: e.scalar_tensor_tensor(
                            a["diag"][:, :], ps[pb][:, :], 1.0,
                            P["masks"][:, (tqc % 4) * 512:(tqc % 4 + 1) * 512],
                            ALU.mult, ALU.add), "evv")
                        dv = c.C["evv"]
                        psum_set_free(c, st, pb, "evv")
                        c.wait("scalar", "evv", dv)
                        c.wait("scalar", "tp", st.get("probs4_tp", 0))
                        c.wait("scalar", "evv", st.get("psums_free_evv", 0))
                        c.op("scalar", lambda e, tq4=tq4, tkb=tkb: e.activation(
                            a["probs4"][:, tq4 * S + tkb * 512:tq4 * S + (tkb + 1) * 512],
                            a["diag"][:, :], AF.Exp, scale=INV_SQRT_D,
                            accum_out=P["psums"][:, tkb:tkb + 1]), "evs")
                        st["diag_evs"] = c.C["evs"]
                    else:
                        c.wait("scalar", "mm", smv)
                        c.wait("scalar", "tp", st.get("probs4_tp", 0))
                        c.wait("scalar", "evv", st.get("psums_free_evv", 0))
                        c.op("scalar", lambda e, pb=pb, tq4=tq4, tkb=tkb: e.activation(
                            a["probs4"][:, tq4 * S + tkb * 512:tq4 * S + (tkb + 1) * 512],
                            ps[pb][:, :], AF.Exp, scale=INV_SQRT_D,
                            accum_out=P["psums"][:, tkb:tkb + 1]), "evs")
                        psum_set_free(c, st, pb, "evs")
                exv = c.C["evs"]
                c.wait("vector", "evs", exv)
                c.op("vector", lambda e, ntk=ntk: e.reduce_sum(
                    P["rcpc"][:, :], P["psums"][:, 0:ntk], AX.X), "evv")
                st["psums_free_evv"] = c.C["evv"]
                c.wait("vector", "evv", c.C["evv"])
                c.op("vector", lambda e: e.reciprocal(P["rcpc"][:, :], P["rcpc"][:, :]), "evv")
                c.wait("vector", "evv", c.C["evv"])
                c.op("vector", lambda e, tq4=tq4, ntk=ntk: e.tensor_scalar(
                    a["probs4"][:, tq4 * S:tq4 * S + ntk * 512],
                    a["probs4"][:, tq4 * S:tq4 * S + ntk * 512],
                    P["rcpc"][:, :1], None, ALU.mult), "evv")
            sm_done = c.C["evv"]
            for kk in range(nkk):
                ptb = st["wtmp_i"] % 2
                c.wait("tensor", "evv", sm_done)
                c.wait("tensor", "evv" if ptb == 0 else "evs", st["pt_free"][ptb])
                for tq4 in range(4):
                    c.op("tensor", lambda e, ptb=ptb, tq4=tq4, kk=kk: e.transpose(
                        pt[ptb][:, tq4 * 128:(tq4 + 1) * 128],
                        a["probs4"][:, tq4 * S + kk * 128:tq4 * S + (kk + 1) * 128],
                        P["ident"][:, :]), "tp" if tq4 == 3 else None)
                tpv = c.C["tp"]
                eng = "vector" if ptb == 0 else "scalar"
                esem = {"vector": "evv", "scalar": "evs"}[eng]
                c.wait(eng, "tp", tpv)
                c.wait(eng, "mm", st.get("pv_mm", 0))
                c.op(eng, lambda e, ptb=ptb, kk=kk: gcopy(
                    e, a["probsT"][:, kk * 512:(kk + 1) * 512], pt[ptb][:, :]), esem)
                st["pt_free"][ptb] = c.C[esem]
                st["wtmp_i"] += 1
            st["probs4_tp"] = c.C["tp"]
            pT_done = {"evv": c.C["evv"], "evs": c.C["evs"]}
            psum_wait(c, st, "tensor", 5)
            wait_marks(c, "tensor", pT_done)
            wait_marks(c, "tensor", vt_done)
            for kk in range(nkk):
                c.op("tensor", lambda e, kk=kk, nkk=nkk: e.matmul(
                    ps[5][:, :], a["vT"][:, kk * 128:(kk + 1) * 128],
                    a["probsT"][:, kk * 512:(kk + 1) * 512],
                    start=(kk == 0), stop=(kk == nkk - 1)), "mm")
            pvv = c.C["mm"]
            st["pv_mm"] = pvv
            c.wait("vector", "mm", pvv)
            c.wait("vector", "ctw", st["ctw_prev"])
            c.op("vector", lambda e, C4=C4: e.tensor_copy(
                a["ctx_sb"][:, C4 * 512:(C4 + 1) * 512], ps[5][:, :]), "evv")
            psum_set_free(c, st, 5, "evv")
        ctx_done = c.C["evv"]
        c.wait("gpsimd", "evv", ctx_done)
        c.wait("gpsimd", "ctw", st["ctw_prev"])
        c.dma("gpsimd", "ctw", dd["ctxh"].ap()[hl * 128:(hl + 1) * 128, :], a["ctx_sb"][:, :])
        st["ctw_prev"] = c.C["ctw"]



def emit_outproj(c, dd, P, o, ps, pt, st):
    c.wait("sync", "ctw", c.C["ctw"])
    for ck in range(8):
        c.dma("sync", "cf", o["ctxf"][:, :].rearrange("p (k t) -> p k t", t=S)[:, ck:ck + 1, :],
              dd["ctxh"].ap()[ck * 128:(ck + 1) * 128, :])
    cfv = c.C["cf"]
    for part in range(4):
        src = bass.AP(dd["outw"], part * 2 * 128 * E,
                      [[E, 128], [128 * E, 2], [1, E]])
        c.dma("gpsimd", "ow", o["outw_sb"][:, :].rearrange(
            "p (k d) -> p k d", d=E)[:, part * 2:(part + 1) * 2, :], src)
    owv = c.C["ow"]
    for tc in range(32):
        for dc in range(8):
            pb = dc % 2
            psum_wait(c, st, "tensor", pb)
            c.wait("tensor", "cf", cfv)
            c.wait("tensor", "ow", owv)
            for ck in range(8):
                c.op("tensor", lambda e, pb=pb, ck=ck, tc=tc, dc=dc: e.matmul(
                    ps[pb][:, :],
                    o["ctxf"][:, ck * S + tc * 128:ck * S + (tc + 1) * 128],
                    o["outw_sb"][:, ck * E + dc * 512:ck * E + (dc + 1) * 512],
                    start=(ck == 0), stop=(ck == 7)), "mm" if ck == 7 else None)
            mmv = c.C["mm"]
            ysl = st["yw_i"] % 4
            c.wait("vector", "mm", mmv)
            c.wait("vector", f"yw{ysl}", st["yw_prev"][ysl])
            c.op("vector", lambda e, pb=pb, ysl=ysl: e.tensor_copy(
                o["ytile"][:, ysl * 512:(ysl + 1) * 512], ps[pb][:, :]), "evv")
            ev = c.C["evv"]
            psum_set_free(c, st, pb, "evv")
            c.wait("gpsimd", "evv", ev)
            c.dma("gpsimd", f"yw{ysl}",
                  dd["rs_in"].ap()[tc * 128:(tc + 1) * 128, dc * 512:(dc + 1) * 512],
                  o["ytile"][:, ysl * 512:(ysl + 1) * 512])
            st["yw_prev"][ysl] = c.C[f"yw{ysl}"]
            st["yw_i"] += 1


def emit_y2_assembly(c, dd, P, o, ps, pt, st):
    c.wait("sync", "mm", c.C["mm"])
    c.dma("sync", "ob", o["obrow"][:, :], dd["outb"].ap())
    obv = c.C["ob"]
    c.wait("tensor", "ob", obv)
    for dc in range(8):
        psum_wait(c, st, "tensor", dc % 2)
        c.op("tensor", lambda e, dc=dc: e.matmul(
            ps[dc % 2][:, :], P["ones_row"][0:1, 0:128],
            o["obrow"][0:1, dc * 512:(dc + 1) * 512], start=True, stop=True), "mm")
        mmv = c.C["mm"]
        c.wait("vector", "mm", mmv)
        c.op("vector", lambda e, dc=dc: e.tensor_copy(
            P["obb"][:, dc * 512:(dc + 1) * 512], ps[dc % 2][:, :]), "evv")
        psum_set_free(c, st, dc % 2, "evv")
    obb_done = c.C["evv"]
    for i in range(4):
        c.wait("sync", f"yw{i}", c.C[f"yw{i}"])
    for half in (0, 1):
        for tc in range(4):
            row0 = half * T + tc * 128
            wait_marks(c, "sync", st["xl_cons"][0])
            c.dma("sync", "xl0", o["yslab"][:, 0:2048],
                  dd["y1h"].ap()[row0:row0 + 128, 0:2048])
            wait_marks(c, "sync", st["xl_cons"][1])
            c.dma("sync", "xl1", o["yslab"][:, 2048:4096],
                  dd["y1h"].ap()[row0:row0 + 128, 2048:4096])
            c.wait("sync", "cc", st["cc_need"])
            c.dma("sync", "xl0", o["rslab"][:, 0:2048],
                  dd["rs_out"].ap()[row0:row0 + 128, 0:2048])
            c.dma("sync", "xl1", o["rslab"][:, 2048:4096],
                  dd["rs_out"].ap()[row0:row0 + 128, 2048:4096])
            xv = {0: c.C["xl0"], 1: c.C["xl1"]}
            c.wait("vector", "xl0", xv[0])
            c.wait("vector", "xl1", xv[1])
            c.wait("vector", "evv", obb_done)
            c.op("vector", lambda e: e.tensor_add(o["yslab"][:, 0:4096],
                                                  o["yslab"][:, 0:4096],
                                                  o["rslab"][:, 0:4096]), "evv")
            c.wait("vector", "evv", c.C["evv"])
            c.op("vector", lambda e: e.tensor_add(o["yslab"][:, 0:4096],
                                                  o["yslab"][:, 0:4096],
                                                  P["obb"][:, :]), "evv")
            y2v = c.C["evv"]
            c.wait("gpsimd", "evv", y2v)
            c.dma("gpsimd", "agw", dd["y2h"].ap()[row0:row0 + 128, :],
                  o["yslab"][:, 0:4096])
            y2wv = c.C["agw"]
            c.wait("vector", "evv", y2v)
            for h8 in range(8):
                c.op("vector", lambda e, h8=h8: e.bn_stats(
                    P["bnst"][:, h8 * 6:(h8 + 1) * 6],
                    o["yslab"][:, h8 * 512:(h8 + 1) * 512]), "evv")
            c.wait("vector", "evv", c.C["evv"])
            c.op("vector", lambda e: e.bn_aggr(P["bnag"][:, :], P["bnst"][:, :]), "evv")
            c.wait("vector", "evv", c.C["evv"])
            c.op("vector", lambda e: e.tensor_scalar(
                P["ln_t0"][:, :], P["bnag"][:, 1:2], LN_EPS, None, ALU.add), "evv")
            c.wait("scalar", "evv", c.C["evv"])
            c.op("scalar", lambda e: e.activation(P["ln_t0"][:, :], P["ln_t0"][:, :],
                                                  AF.Sqrt), "evs")
            c.wait("vector", "evs", c.C["evs"])
            c.op("vector", lambda e: e.reciprocal(P["ln_r"][:, :], P["ln_t0"][:, :]), "evv")
            c.wait("vector", "evv", c.C["evv"])
            c.op("vector", lambda e: e.scalar_tensor_tensor(
                P["ln_nmr"][:, :], P["bnag"][:, 0:1], -1.0, P["ln_r"][:, :],
                ALU.mult, ALU.mult), "evv")
            c.wait("scalar", "evv", c.C["evv"])
            c.wait("scalar", "agw", st.get("xntm3_agw", 0))
            c.op("scalar", lambda e: e.activation(
                o["xntm"][:, :], o["yslab"][:, 0:4096], AF.Identity,
                bias=P["ln_nmr"][:, :1], scale=P["ln_r"][:, :1]), "evs")
            apv = c.C["evs"]
            c.wait("gpsimd", "evs", apv)
            c.dma("gpsimd", "agw", dd["xn3h"].ap()[row0:row0 + 128, :], o["xntm"][:, :])
            st["xntm3_agw"] = c.C["agw"]
            st["xl_cons"][0] = {"evs": apv, "agw": st["xntm3_agw"], "evv": c.C["evv"]}
            st["xl_cons"][1] = {"evs": apv, "agw": st["xntm3_agw"], "evv": c.C["evv"]}



def emit_all(ctx, dd, P, L, ps, pt):
    c = ctx
    st = new_state()
    st["yw_prev"] = {i: 0 for i in range(4)}
    st["agev_prev"] = {0: 0, 1: 0}
    st["qk_prev_mm"] = 0
    st["ctw_prev"] = 0
    st["cc_need"] = 0

    L.phase_reset()
    f = {}
    f["xnt"] = L.scratch("xnt", [128, NE * T], BF16)
    hbuf_off = L.cursor
    f["hbuf"] = L.scratch("hbuf", [128, NI * T], BF16)
    f["wst"] = L.scratch("wst", [128, 4 * 512], BF16)
    f["qbuf"] = L.scratch("qbuf", [128, 2 * 1024], BF16)
    f["sczs"] = L.scratch("sczs", [128, 4 * 288], F32)
    f["dqs"] = L.scratch("dqs", [128, 4 * 1024], BF16)
    f["wtmp"] = L.scratch("wtmp", [128, 2 * 512], BF16)
    f["xys"] = L.scratch("xys", [128, 2 * 2048], F32)
    f["ytile"] = L.scratch("ytile", [128, 4 * 512], F32)
    f["xntm"] = L.scratch("xntm", [128, 4096], BF16)
    P["rope32"] = L._alloc("rope32", [128, S], F32, hbuf_off)
    P["rope32b"] = L._alloc("rope32b", [128, S], F32, hbuf_off + 16384)

    a = {}
    L.phase_reset()
    a["qkvw_sb"] = L.scratch("qkvw_sb", [128, NE * 384], BF16)
    a["qkv_sb"] = L.scratch("qkv_sb", [128, 3 * S], BF16)
    a["l2slab"] = L.scratch("l2slab", [128, NE * 512], BF16)
    a["diag"] = L.scratch("diag", [128, 512], F32)
    a["probs4"] = L.scratch("probs4", [128, 4 * S], BF16)
    a["probsT"] = L.scratch("probsT", [128, 32 * 512], BF16)
    a["vT"] = L.scratch("vT", [128, 32 * 128], BF16)
    a["ctx_sb"] = L.scratch("ctx_sb", [128, S], BF16)
    a["qkvb_sb"] = L.scratch("qkvb_sb", [1, 384], F32)

    o = {}
    L.phase_reset()
    o["ctxf"] = L.scratch("ctxf", [128, 8 * S], BF16)
    o["outw_sb"] = L.scratch("outw_sb", [128, 8 * E], BF16)
    o["ytile"] = L.scratch("ytileo", [128, 4 * 512], F32)
    L.phase_reset()
    o["yslab"] = L.scratch("yslab", [128, 2 * 2048], F32)
    o["rslab"] = L.scratch("rslab", [128, 4096], F32)
    o["xntm"] = L.scratch("xntmo", [128, 4096], BF16)
    o["obrow"] = L.scratch("obrow", [1, E], F32)
    P["obb"] = L.scratch("obb", [128, E], F32)

    emit_prologue(c, dd, P)
    emit_ffn_block(c, dd, P, f, ps, pt, st, first=True)
    c.wait("gpsimd", "agw", c.C["agw"])
    c.collective("cc", "AllGather", ALU.bypass,
                 [dd["ag1_in"].ap().opt()], [dd["ag1_out"].ap().opt()])
    st["cc_need"] = c.C["cc"]
    emit_attention(c, dd, P, a, ps, pt, st)
    emit_outproj(c, dd, P, o, ps, pt, st)
    for i in range(4):
        c.wait("gpsimd", f"yw{i}", c.C[f"yw{i}"])
    c.collective("cc", "ReduceScatter", ALU.add,
                 [dd["rs_in"].ap().opt()], [dd["rs_out"].ap().opt()])
    st["cc_need"] = c.C["cc"]
    emit_y2_assembly(c, dd, P, o, ps, pt, st)
    emit_ffn_block(c, dd, P, f, ps, pt, st, first=False)
    for i in range(4):
        c.wait("gpsimd", f"yw{i}", c.C[f"yw{i}"])



_NC_CACHE = {}


def kernel(**inputs):
    x = np.ascontiguousarray(np.asarray(inputs["L0_attn_hidden_states"], dtype=np.float32))
    qkv_w = np.asarray(inputs["L0_attn_qkv_w"], dtype=np.float32)
    qkv_b = np.asarray(inputs["L0_attn_qkv_b"], dtype=np.float32)
    out_w = np.asarray(inputs["L0_attn_out_w"], dtype=np.float32)
    out_b = np.asarray(inputs["L0_attn_out_b"], dtype=np.float32)
    gw = np.ascontiguousarray(np.asarray(inputs["L0_ffn_gate_qweight"], dtype=np.int32))
    gs = np.ascontiguousarray(np.asarray(inputs["L0_ffn_gate_scales"], dtype=np.float32))
    gz = np.ascontiguousarray(np.asarray(inputs["L0_ffn_gate_zeros"], dtype=np.float32))
    uw = np.ascontiguousarray(np.asarray(inputs["L0_ffn_up_qweight"], dtype=np.int32))
    us = np.ascontiguousarray(np.asarray(inputs["L0_ffn_up_scales"], dtype=np.float32))
    uz = np.ascontiguousarray(np.asarray(inputs["L0_ffn_up_zeros"], dtype=np.float32))
    dw = np.ascontiguousarray(np.asarray(inputs["L0_ffn_down_qweight"], dtype=np.int32))
    ds_ = np.ascontiguousarray(np.asarray(inputs["L0_ffn_down_scales"], dtype=np.float32))
    dz = np.ascontiguousarray(np.asarray(inputs["L0_ffn_down_zeros"], dtype=np.float32))

    if "nc" not in _NC_CACHE:
        _NC_CACHE["nc"] = build_nc()
    nc = _NC_CACHE["nc"]

    in_maps = []
    for cid in range(NCORES):
        b, qt = cid // 4, cid % 4
        hg0 = 8 * (cid % 4)
        qcols = []
        bcols = []
        for hl in range(8):
            h = hg0 + hl
            qcols.append(qkv_w[:, h * 128:(h + 1) * 128])
            qcols.append(qkv_w[:, E + h * 128:E + (h + 1) * 128])
            qcols.append(qkv_w[:, 2 * E + h * 128:2 * E + (h + 1) * 128])
            bcols.append(np.concatenate(
                [qkv_b[h * 128:(h + 1) * 128],
                 qkv_b[E + h * 128:E + (h + 1) * 128],
                 qkv_b[2 * E + h * 128:2 * E + (h + 1) * 128]]))
        qkvw_own = np.ascontiguousarray(np.concatenate(qcols, axis=1))
        qkvb_own = np.ascontiguousarray(np.stack(bcols, axis=0))
        outw_own = np.ascontiguousarray(out_w[hg0 * 128:(hg0 + 8) * 128, :])
        in_maps.append(dict(
            x_own=np.ascontiguousarray(x[b, qt * TOK:(qt + 1) * TOK, :]),
            qw_g=gw, sc_g=gs, zr_g=gz,
            qw_u=uw, sc_u=us, zr_u=uz,
            qw_d=dw, sc_d=ds_, zr_d=dz,
            qkvw=qkvw_own, qkvb=qkvb_own,
            outw=outw_own, outb=np.ascontiguousarray(out_b.reshape(8, 512)),
        ))

    res = run_bass_kernel_spmd(nc, in_maps, core_ids=list(range(NCORES)))
    out = np.empty((B, S, E), dtype=np.float32)
    for cid in range(NCORES):
        b, qt = cid // 4, cid % 4
        out[b, qt * TOK:(qt + 1) * TOK, :] = res.results[cid]["y_out"]
    return out


if __name__ == "__main__":
    nc = build_nc()
    n = sum(1 for _ in nc.all_instructions())
    print("instructions:", n)
